# version 02
# baseline (speedup 1.0000x reference)
import numpy as np
from contextlib import ExitStack

import concourse.bass as bass
import concourse.mybir as mybir
from concourse.bass_utils import run_bass_kernel_spmd

F32 = mybir.dt.float32
AF = mybir.ActivationFunctionType
OP = mybir.AluOpType
AX = mybir.AxisListType

B, C, H, S = 8, 1024, 16, 64
KC = C // 128
E = 8
F2 = 2048
LN_EPS = 1e-5
ALPHA, LIMIT = 1.702, 7.0
NEG_BIG = -1.0e30

TT = 512
RING = 2
UNITS = 98


def _pack_lhsT(WT):
    K, M = WT.shape
    L = WT.reshape(K // 128, 128, M // 128, 128).transpose(2, 1, 0, 3)
    return np.ascontiguousarray(L)


def _unit4(L4):
    n = L4.shape[0]
    X = L4.shape[1] * 0 + L4.shape[2]
    return np.ascontiguousarray(L4.transpose(1, 0, 2).reshape(128, n * X))


def _perm_f():
    ge = np.arange(0, F2, 2)
    le = np.arange(1, F2, 2)
    out = []
    for i in range(8):
        out.append(ge[128 * i:128 * (i + 1)])
        out.append(le[128 * i:128 * (i + 1)])
    return np.concatenate(out)


def _colpack(v):
    return np.ascontiguousarray(np.asarray(v, np.float32).reshape(KC, 128).T)


def build_wpack(ins):
    perm = _perm_f()
    units = []
    for nm in ("Wk", "Wv", "Wr", "Wo"):
        L = _pack_lhsT(np.ascontiguousarray(ins["L0_attn_" + nm].T))
        units.append(_unit4(L[0:4].reshape(4, 128, KC * 128)))
        units.append(_unit4(L[4:8].reshape(4, 128, KC * 128)))
    for e in range(9):
        if e < E:
            w1 = ins["L0_ffn_w1"][e][perm]
            w3 = ins["L0_ffn_w3"][e][perm]
            w2 = ins["L0_ffn_w2"][e]
        else:
            w1 = ins["L0_ffn_sw1"][perm]
            w3 = ins["L0_ffn_sw3"][perm]
            w2 = ins["L0_ffn_sw2"]
        L1 = _pack_lhsT(np.ascontiguousarray(w1.T)).reshape(16, 128, KC * 128)
        L3 = _pack_lhsT(np.ascontiguousarray(w3.T)).reshape(16, 128, KC * 128)
        for i in range(8):
            quad = np.stack([L1[2 * i], L1[2 * i + 1], L3[2 * i], L3[2 * i + 1]])
            units.append(_unit4(quad))
        L2 = _pack_lhsT(np.ascontiguousarray(w2.T)).reshape(8, 128, KC * 128)
        units.append(_unit4(L2[0:4]))
        units.append(_unit4(L2[4:8]))
    return np.ascontiguousarray(np.stack(units))


def build_smalls(ins):
    perm = _perm_f()
    b1c, b3c = [], []
    for e in range(E):
        b1c.append(ins["L0_ffn_b1"][e][perm])
        b3c.append(ins["L0_ffn_b3"][e][perm])
    b1c.append(ins["L0_ffn_sb1"][perm])
    b3c.append(ins["L0_ffn_sb3"][perm])
    b1c = np.concatenate(
        [np.asarray(b, np.float32).reshape(16, 128).T for b in b1c], axis=1)
    b3c = np.concatenate(
        [np.asarray(b, np.float32).reshape(16, 128).T for b in b3c], axis=1)
    b2c = np.concatenate(
        [np.asarray(ins["L0_ffn_b2"][e], np.float32).reshape(8, 128).T
         for e in range(E)], axis=1)
    gw = np.ascontiguousarray(
        np.asarray(ins["L0_ffn_gate_w"], np.float32).T
        .reshape(KC, 128, E).transpose(1, 0, 2).reshape(128, KC * E))
    ind = np.zeros((E, E * 128), np.float32)
    for e in range(E):
        ind[e, e * 128:(e + 1) * 128] = 1.0
    return {
        "mixk": _colpack(ins["L0_attn_time_mix_k"].reshape(-1)),
        "mixv": _colpack(ins["L0_attn_time_mix_v"].reshape(-1)),
        "mixr": _colpack(ins["L0_attn_time_mix_r"].reshape(-1)),
        "decr": _colpack(ins["L0_attn_time_decay"].reshape(-1)),
        "tfc": _colpack(ins["L0_attn_time_first"].reshape(-1)),
        "lng": _colpack(ins["L0_attn_ln_g"]),
        "lnb": _colpack(ins["L0_attn_ln_b"]),
        "gw": gw,
        "gb": np.asarray(ins["L0_ffn_gate_b"], np.float32)[None, :],
        "b1c": np.ascontiguousarray(b1c),
        "b3c": np.ascontiguousarray(b3c),
        "b2c": np.ascontiguousarray(b2c),
        "sb2c": _colpack(ins["L0_ffn_sb2"]),
        "ind": ind,
        "eye8": np.eye(8, dtype=np.float32),
        "eye128": np.eye(128, dtype=np.float32),
    }


class Buf:
    __slots__ = ("w", "r")

    def __init__(self):
        self.w = None
        self.r = {}


class Prog:

    def __init__(self, nc, sem_names):
        self.nc = nc
        self.streams = {"t": [], "v": [], "s": [], "g": [], "y": []}
        self.cnt = {n: 0 for n in sem_names}
        self.wm = {e: {} for e in self.streams}
        self.sems = {}

    def emit(self, stream, sem, inc, reads, writes, fn, skip_waw=False):
        waits = {}

        def need(s2, v2):
            if v2 and v2 > waits.get(s2, 0):
                waits[s2] = v2

        for b in reads:
            if b.w:
                need(*b.w)
        for b in writes:
            if b.w and not skip_waw:
                need(*b.w)
            for s2, v2 in b.r.items():
                need(s2, v2)
        wm = self.wm[stream]
        waits = {s2: v2 for s2, v2 in waits.items() if v2 > wm.get(s2, 0)}
        for s2, v2 in waits.items():
            wm[s2] = v2
        self.cnt[sem] += inc
        mark = self.cnt[sem]
        wl = sorted(waits.items())
        sems = self.sems

        def closure(eng, wl=wl, fn=fn, sem=sem, inc=inc):
            for s2, v2 in wl:
                eng.wait_ge(sems[s2], v2)
            inst = fn(eng)
            inst.then_inc(sems[sem], inc)

        self.streams[stream].append(closure)
        for b in writes:
            b.w = (sem, mark)
            b.r = {}
        for b in reads:
            if b.r.get(sem, 0) < mark:
                b.r[sem] = mark
        return mark

    def t(self, reads, writes, fn, skip_waw=True):
        return self.emit("t", "t", 1, reads, writes, fn, skip_waw=skip_waw)

    def v(self, reads, writes, fn):
        return self.emit("v", "v", 1, reads, writes, fn)

    def s(self, reads, writes, fn):
        return self.emit("s", "s", 1, reads, writes, fn)

    def dma(self, stream, sem, reads, writes, fn):
        return self.emit(stream, sem, 16, reads, writes, fn)

    def raw(self, stream, fn):
        self.streams[stream].append(fn)


DEBUG_TAPS = []
DEBUG_TILE = [0]


def build_program(T):
    NT = T // TT
    nc = bass.Bass("TRN2", target_bir_lowering=False)

    x_h = nc.dram_tensor("x", [C, T], F32, kind="ExternalInput").ap()
    y_h = nc.dram_tensor("y", [C, T], F32, kind="ExternalOutput").ap()
    wpack = nc.dram_tensor("wpack", [UNITS, 128, 4096], F32,
                           kind="ExternalInput").ap()
    sm_h = {}
    sm_shapes = {
        "mixk": [128, 8], "mixv": [128, 8], "mixr": [128, 8],
        "decr": [128, 8], "tfc": [128, 8], "lng": [128, 8], "lnb": [128, 8],
        "gw": [128, KC * E], "gb": [1, 8], "b1c": [128, 144],
        "b3c": [128, 144], "b2c": [128, 64], "sb2c": [128, 8],
        "ind": [8, 1024], "eye8": [8, 8], "eye128": [128, 128],
    }
    for nm, shp in sm_shapes.items():
        sm_h[nm] = nc.dram_tensor(nm, shp, F32, kind="ExternalInput").ap()

    def sb(name, shape):
        return nc.alloc_sbuf_tensor(name, shape, F32).ap()

    W = KC * TT
    x_t = sb("x_t", [128, W])
    ln_t = sb("ln_t", [128, W])
    sh_t = sb("sh_t", [128, W])
    dec_c_h = nc.alloc_sbuf_tensor("dec_c", [128, 8], F32)
    scr1 = sb("scr1", [128, W])
    mix_t = sb("mix_t", [128, W])
    k_t = sb("k_t", [128, W])
    kv_t = sb("kv_t", [128, W])
    r_t = sb("r_t", [128, W])
    s_t = sb("s_t", [128, W])
    a_t = scr1
    acc = sh_t
    ring = [sb(f"ring{i}", [128, 4096]) for i in range(RING)]
    hh1 = sb("hh1", [128, TT])
    hh2 = sb("hh2", [128, TT])
    hg_s = sb("hg_s", [128, TT])
    hl_s = sb("hl_s", [128, TT])
    sg_s = sb("sg_s", [128, TT])
    cwb_sb = sb("cwb_sb", [128, TT])
    g_sb = sb("g_sb", [8, TT])
    gt_s = sb("gt_s", [128, (TT // 128) * 8])
    cwt_s = sb("cwt_s", [128, (TT // 128) * 8])
    cwem = sb("cwem", [8, TT])
    grow = sb("grow", [1, 4 * TT])
    sml = {nm: sb("s_" + nm, shp) for nm, shp in sm_shapes.items()}
    dec_c = dec_c_h.ap()
    gbb = sb("gbb", [128, 8])
    car_ln = [sb("car_ln0", [128, 8]), sb("car_ln1", [128, 8])]
    car_sc = sb("car_sc", [128, 8])
    ones_c = sb("ones_c", [128, 1])
    ones_l = sb("ones_l", [1, 128])
    eps_c = sb("eps_c", [1, 1])

    pb = [nc.alloc_psum_tensor(f"pb{i}", [128, 512], F32).ap()
          for i in range(8)]

    SEMS = ["t", "v", "s", "qx", "qy", "qg"] + [f"qs{i}" for i in range(RING)] + [f"qd{i}" for i in range(6)]
    p = Prog(nc, SEMS)
    bufs = {}

    _ALIAS = {"a_t": "scr1", "acc": "sh_t"}

    def Bf(name):
        name = _ALIAS.get(name, name)
        if name not in bufs:
            bufs[name] = Buf()
        return bufs[name]

    with ExitStack() as ctx:
        for s2 in SEMS:
            p.sems[s2] = ctx.enter_context(nc.semaphore("sem_" + s2))
        block = ctx.enter_context(nc.Block())

        for nm in sm_shapes:
            p.dma("g", "qg", [], [Bf("sm_" + nm)],
                  lambda e, nm=nm: e.dma_start(out=sml[nm], in_=sm_h[nm]))
        for nm in sm_shapes:
            bufs["sm_" + nm].w = ("qg", p.cnt["qg"])
        p.v([], [Bf("ones")], lambda e: e.memset(ones_c, 1.0))
        p.v([], [Bf("ones")], lambda e: e.memset(ones_l, 1.0))
        p.v([], [Bf("eps")], lambda e: e.memset(eps_c, LN_EPS))
        p.v([], [Bf("car_ln0")], lambda e: e.memset(car_ln[0], 0.0))
        p.v([], [Bf("car_ln1")], lambda e: e.memset(car_ln[1], 0.0))
        p.v([], [Bf("car_sc")], lambda e: e.memset(car_sc, 0.0))
        p.s([Bf("sm_decr")], [Bf("dec_c")],
            lambda e: e.activation(out=dec_c, in_=sml["decr"], func=AF.Exp))
        p.t([Bf("ones"), Bf("sm_gb")], [Bf("pb7")],
            lambda e: e.matmul(pb[7][:, 0:8], ones_l, sml["gb"],
                               start=True, stop=True))
        p.v([Bf("pb7")], [Bf("gbb")],
            lambda e: e.tensor_copy(gbb, pb[7][:, 0:8]))

        unit_state = {"emitted": 0}

        def unit_buf(u):
            return ring[u % RING], Bf(f"ring{u % RING}")

        def want_unit(u):
            hi = min(u + RING, NT * UNITS)
            while unit_state["emitted"] < hi:
                uu = unit_state["emitted"]
                rb, rbuf = unit_buf(uu)
                p.dma("y", f"qs{uu % RING}", [], [rbuf],
                      lambda e, uu=uu, rb=rb:
                      e.dma_start(out=rb, in_=wpack[uu % UNITS]))
                unit_state["emitted"] += 1

        def mm_group(bank_i, out_ap, lhs_fn, rhs_reads, rhs_fn, nk,
                     extra=None):
            reads = list(rhs_reads)
            reads.extend(lhs_fn[1])
            bank = Bf(f"pb{bank_i}")

            def fn(e):
                inst = None
                for kk in range(nk):
                    inst = e.matmul(out_ap, lhs_fn[0](kk), rhs_fn(kk),
                                    start=(kk == 0),
                                    stop=(kk == nk - 1 and extra is None))
                if extra is not None:
                    inst = extra(e)
                return inst
            return p.t(reads, [bank], fn)

        def chunk(t, kc):
            return t[:, kc * TT:(kc + 1) * TT]

        GR = {}
        for i, nm in enumerate(("mean", "msq", "rstd", "negm")):
            GR[nm] = grow[0:1, i * TT:(i + 1) * TT]
        GR["var"] = GR["msq"]
        grow_b = Bf("grow")

        def layer_norm(src, src_b, dst, dst_b, affine):
            p.s([src_b], [Bf("scr1")],
                lambda e: e.activation(out=scr1, in_=src, func=AF.Square))
            st6 = Bf("pb6")

            def fsum(e):
                for kc in range(KC):
                    e.matmul(pb[6][0:1, 0:TT], ones_c, chunk(src, kc),
                             start=(kc == 0), stop=(kc == KC - 1))
                for kc in range(KC):
                    inst = e.matmul(pb[6][32:33, 0:TT], ones_c,
                                    chunk(scr1, kc),
                                    start=(kc == 0), stop=(kc == KC - 1))
                return inst
            p.t([src_b, Bf("scr1"), Bf("ones")], [st6], fsum)
            p.v([st6], [grow_b], lambda e: e.tensor_scalar_mul(
                GR["mean"], pb[6][0:1, 0:TT], 1.0 / C))
            p.v([st6, grow_b], [grow_b], lambda e: e.tensor_scalar_mul(
                GR["msq"], pb[6][32:33, 0:TT], 1.0 / C))
            p.v([grow_b], [grow_b], lambda e: e.tensor_tensor(
                GR["rstd"], GR["mean"], GR["mean"], OP.mult))
            p.v([grow_b], [grow_b], lambda e: e.tensor_tensor(
                GR["var"], GR["msq"], GR["rstd"], OP.subtract))
            p.s([grow_b, Bf("eps")], [grow_b], lambda e: e.activation(
                out=GR["rstd"], in_=GR["var"], func=AF.Sqrt, bias=eps_c))
            p.v([grow_b], [grow_b],
                lambda e: e.reciprocal(GR["rstd"], GR["rstd"]))
            p.v([grow_b], [grow_b], lambda e: e.scalar_tensor_tensor(
                GR["negm"], GR["mean"], -1.0, GR["rstd"], OP.mult, OP.mult))
            p.t([grow_b, Bf("ones")], [Bf("pb4")],
                lambda e: e.matmul(pb[4][:, 0:TT], ones_l, GR["rstd"],
                                   start=True, stop=True))
            p.t([grow_b, Bf("ones")], [Bf("pb5")],
                lambda e: e.matmul(pb[5][:, 0:TT], ones_l, GR["negm"],
                                   start=True, stop=True))
            for kc in range(KC):
                p.v([src_b, Bf("pb4")], [dst_b],
                    lambda e, kc=kc: e.tensor_tensor(
                        chunk(dst, kc), chunk(src, kc), pb[4][:, 0:TT],
                        OP.mult))
                p.v([dst_b, Bf("pb5")], [dst_b],
                    lambda e, kc=kc: e.tensor_tensor(
                        chunk(dst, kc), chunk(dst, kc), pb[5][:, 0:TT],
                        OP.add))
                if affine:
                    p.v([dst_b, Bf("sm_lng"), Bf("sm_lnb")], [dst_b],
                        lambda e, kc=kc: e.tensor_scalar(
                            chunk(dst, kc), chunk(dst, kc),
                            sml["lng"][:, kc:kc + 1],
                            sml["lnb"][:, kc:kc + 1], OP.mult, OP.add))

        def bcols(t, col):
            return t[:, col::TT]

        for g in range(NT):
            t0 = g * TT
            u0 = g * UNITS
            carA, carB = car_ln[g % 2], car_ln[(g + 1) % 2]
            carA_b, carB_b = Bf(f"car_ln{g % 2}"), Bf(f"car_ln{(g + 1) % 2}")

            p.dma("g", "qx", [], [Bf("x_t")],
                  lambda e, t0=t0: e.dma_start(
                        out=x_t.rearrange("p (k t) -> p k t", k=KC),
                        in_=x_h.rearrange("(k p) t -> p k t",
                                          k=KC)[:, :, t0:t0 + TT]))

            layer_norm(x_t, Bf("x_t"), ln_t, Bf("ln_t"), affine=False)
            p.v([Bf("ln_t")], [carB_b],
                lambda e, carB=carB: e.tensor_copy(carB, bcols(ln_t, TT - 1)))
            p.v([Bf("ln_t")], [Bf("sh_t")],
                lambda e: e.tensor_copy(sh_t[:, 1:W], ln_t[:, 0:W - 1]))
            p.v([carA_b, Bf("sh_t")], [Bf("sh_t")],
                lambda e, carA=carA: e.tensor_copy(bcols(sh_t, 0), carA))
            p.v([Bf("ln_t"), Bf("sh_t")], [Bf("scr1")],
                lambda e: e.tensor_tensor(scr1, ln_t, sh_t, OP.subtract))
            if "lnsh" in DEBUG_TAPS and g == DEBUG_TILE[0]:
                dbg4 = nc.dram_tensor("dbg_ln", [128, W], F32,
                                      kind="ExternalOutput").ap()
                dbg5 = nc.dram_tensor("dbg_sh", [128, W], F32,
                                      kind="ExternalOutput").ap()
                p.dma("g", "qd3", [Bf("ln_t")], [Bf("dbg4")],
                      lambda e: e.dma_start(out=dbg4, in_=ln_t))
                p.dma("g", "qd4", [Bf("sh_t")], [Bf("dbg5")],
                      lambda e: e.dma_start(out=dbg5, in_=sh_t))

            for pi, (mixnm, kind) in enumerate(
                    (("mixk", "k"), ("mixv", "v"), ("mixr", "r"))):
                for kc in range(KC):
                    p.v([Bf("scr1"), Bf("sh_t"), Bf("sm_" + mixnm)],
                        [Bf("mix_t")],
                        lambda e, kc=kc, mixnm=mixnm: e.scalar_tensor_tensor(
                            chunk(mix_t, kc), chunk(scr1, kc),
                            sml[mixnm][:, kc:kc + 1], chunk(sh_t, kc),
                            OP.mult, OP.add))
                for dc in range(KC):
                    u = u0 + 2 * pi + dc // 4
                    want_unit(u)
                    rb, rbuf = unit_buf(u)
                    sub = dc % 4
                    bi = dc % 2
                    mm_group(
                        bi, pb[bi][:, 0:TT],
                        (lambda kk, rb=rb, sub=sub:
                         rb[:, sub * 1024 + kk * 128:sub * 1024 + (kk + 1) * 128],
                         [rbuf]),
                        [Bf("mix_t")],
                        lambda kk: chunk(mix_t, kk), KC)
                    if kind == "k":
                        p.s([Bf(f"pb{bi}")], [Bf("k_t")],
                            lambda e, dc=dc, bi=bi: e.activation(
                                out=chunk(k_t, dc), in_=pb[bi][:, 0:TT],
                                func=AF.Copy))
                    elif kind == "v":
                        p.v([Bf(f"pb{bi}"), Bf("k_t")], [Bf("kv_t")],
                            lambda e, dc=dc, bi=bi: e.tensor_tensor(
                                chunk(kv_t, dc), chunk(k_t, dc),
                                pb[bi][:, 0:TT], OP.mult))
                    else:
                        p.s([Bf(f"pb{bi}")], [Bf("r_t")],
                            lambda e, dc=dc, bi=bi: e.activation(
                                out=chunk(r_t, dc), in_=pb[bi][:, 0:TT],
                                func=AF.Copy))

            for kc in range(KC):
                dec_b = bass.AP(dec_c_h, kc, [[8, 128], [0, TT]])
                p.v([Bf("kv_t"), Bf("dec_c"), Bf("car_sc")], [Bf("s_t")],
                    lambda e, kc=kc, dec_b=dec_b: e.tensor_tensor_scan(
                        chunk(s_t, kc), dec_b, chunk(kv_t, kc),
                        car_sc[:, kc:kc + 1], OP.mult, OP.add))
            p.v([Bf("s_t")], [Bf("car_sc")],
                lambda e: e.tensor_copy(car_sc, bcols(s_t, TT - 1)))

            for kc in range(KC):
                p.v([Bf("kv_t"), Bf("s_t"), Bf("sm_tfc")], [Bf("s_t")],
                    lambda e, kc=kc: e.scalar_tensor_tensor(
                        chunk(s_t, kc), chunk(kv_t, kc),
                        sml["tfc"][:, kc:kc + 1], chunk(s_t, kc),
                        OP.mult, OP.add))
            for kc in range(KC):
                p.v([Bf("s_t"), Bf("r_t")], [Bf("kv_t")],
                    lambda e, kc=kc: e.tensor_tensor(
                        chunk(kv_t, kc), chunk(r_t, kc), chunk(s_t, kc),
                        OP.mult))

            layer_norm(kv_t, Bf("kv_t"), ln_t, Bf("ln_t"), affine=True)

            for dc in range(KC):
                u = u0 + 6 + dc // 4
                want_unit(u)
                rb, rbuf = unit_buf(u)
                sub = dc % 4
                bi = dc % 2
                mm_group(
                    bi, pb[bi][:, 0:TT],
                    (lambda kk, rb=rb, sub=sub:
                     rb[:, sub * 1024 + kk * 128:sub * 1024 + (kk + 1) * 128],
                     [rbuf]),
                    [Bf("ln_t")],
                    lambda kk: chunk(ln_t, kk), KC)
                p.v([Bf(f"pb{bi}"), Bf("x_t")], [Bf("x_t")],
                    lambda e, dc=dc, bi=bi: e.tensor_tensor(
                        chunk(x_t, dc), chunk(x_t, dc), pb[bi][:, 0:TT],
                        OP.add))

            if "xattn" in DEBUG_TAPS and g == DEBUG_TILE[0]:
                dbg = nc.dram_tensor("dbg_xattn", [128, W], F32,
                                     kind="ExternalOutput").ap()
                p.dma("g", "qd0", [Bf("x_t")], [Bf("dbg1")],
                      lambda e: e.dma_start(out=dbg, in_=x_t))

            layer_norm(x_t, Bf("x_t"), ln_t, Bf("ln_t"), affine=False)

            def fgate(e):
                for kc in range(KC):
                    inst = e.matmul(pb[5][0:8, 0:TT],
                                    sml["gw"][:, kc * 8:(kc + 1) * 8],
                                    chunk(ln_t, kc),
                                    start=(kc == 0), stop=(kc == KC - 1))
                return inst
            p.t([Bf("ln_t"), Bf("sm_gw")], [Bf("pb5")], fgate)
            p.v([Bf("pb5")], [Bf("g_sb")],
                lambda e: e.tensor_copy(g_sb, pb[5][0:8, 0:TT]))
            nsub = TT // 128
            for sub in range(nsub):
                p.t([Bf("g_sb"), Bf("sm_eye8")], [Bf("pb6")],
                    lambda e, sub=sub: e.transpose(
                        pb[6][:, 0:8], g_sb[:, sub * 128:(sub + 1) * 128],
                        sml["eye8"]))
                p.v([Bf("pb6")], [Bf("gt_s")],
                    lambda e, sub=sub: e.tensor_copy(
                        gt_s[:, sub * 8:(sub + 1) * 8], pb[6][:, 0:8]))
            for sub in range(nsub):
                gts = gt_s[:, sub * 8:(sub + 1) * 8]
                col = grow[0:1, 0:1]
                m1 = cwt_s[:, sub * 8 + 0: sub * 8 + 1]
                c_m = sg_s[:, 0:1]
                c_s = sg_s[:, 1:2]
                et = hh1[:, 0:8]
                bt = hh2[:, 0:8]
                p.v([Bf("gt_s")], [Bf("sg_s")],
                    lambda e, gts=gts, c_m=c_m: e.tensor_reduce(
                        c_m, gts, AX.X, OP.max))
                p.v([Bf("sg_s")], [Bf("sg_s")],
                    lambda e, c_m=c_m: e.tensor_scalar_mul(c_m, c_m, -1.0))
                p.s([Bf("gt_s"), Bf("sg_s")], [Bf("hh1")],
                    lambda e, gts=gts, et=et, c_m=c_m: e.activation(
                        out=et, in_=gts, func=AF.Exp, bias=c_m))
                p.v([Bf("hh1")], [Bf("sg_s")],
                    lambda e, et=et, c_s=c_s: e.tensor_reduce(
                        c_s, et, AX.X, OP.add))
                p.v([Bf("sg_s")], [Bf("sg_s")],
                    lambda e, c_s=c_s: e.reciprocal(c_s, c_s))
                p.v([Bf("hh1"), Bf("sg_s")], [Bf("hh1")],
                    lambda e, et=et, c_s=c_s: e.tensor_scalar_mul(
                        et, et, c_s))
                p.v([Bf("hh1"), Bf("gbb")], [Bf("hh2")],
                    lambda e, et=et, bt=bt: e.tensor_tensor(
                        bt, et, gbb, OP.add))
                p.v([Bf("hh2")], [Bf("sg_s")],
                    lambda e, bt=bt, c_m=c_m: e.tensor_reduce(
                        c_m, bt, AX.X, OP.max))
                p.v([Bf("hh2"), Bf("sg_s")], [Bf("hl_s")],
                    lambda e, bt=bt, c_m=c_m: e.tensor_scalar(
                        hl_s[:, 0:8], bt, c_m, None, OP.is_equal))
                p.v([Bf("hh2"), Bf("hl_s")], [Bf("hh2")],
                    lambda e, bt=bt: e.scalar_tensor_tensor(
                        bt, hl_s[:, 0:8], NEG_BIG, bt, OP.mult, OP.add))
                p.v([Bf("hh2")], [Bf("sg_s")],
                    lambda e, bt=bt, c_s=c_s: e.tensor_reduce(
                        c_s, bt, AX.X, OP.max))
                p.v([Bf("hh2"), Bf("sg_s")], [Bf("hg_s")],
                    lambda e, bt=bt, c_s=c_s: e.tensor_scalar(
                        hg_s[:, 0:8], bt, c_s, None, OP.is_equal))
                p.v([Bf("hl_s"), Bf("hg_s")], [Bf("hl_s")],
                    lambda e: e.tensor_tensor(
                        hl_s[:, 0:8], hl_s[:, 0:8], hg_s[:, 0:8], OP.add))
                p.v([Bf("hl_s"), Bf("hh1")], [Bf("cwt_s")],
                    lambda e, sub=sub, et=et: e.tensor_tensor(
                        cwt_s[:, sub * 8:(sub + 1) * 8], hl_s[:, 0:8], et,
                        OP.mult))
            for sub in range(nsub):
                p.t([Bf("cwt_s"), Bf("sm_eye128")], [Bf("pb6")],
                    lambda e, sub=sub: e.transpose(
                        pb[6][0:8, 0:128],
                        cwt_s[:, sub * 8:(sub + 1) * 8], sml["eye128"]))
                p.v([Bf("pb6")], [Bf("cwem")],
                    lambda e, sub=sub: e.tensor_copy(
                        cwem[:, sub * 128:(sub + 1) * 128],
                        pb[6][0:8, 0:128]))

            if "cwem" in DEBUG_TAPS and g == DEBUG_TILE[0]:
                dbg2 = nc.dram_tensor("dbg_cwem", [8, TT], F32,
                                      kind="ExternalOutput").ap()
                p.dma("g", "qd1", [Bf("cwem")], [Bf("dbg2")],
                      lambda e: e.dma_start(out=dbg2, in_=cwem))

            for ex in range(9):
                eu = u0 + 8 + ex * 10
                routed = ex < E
                if routed:
                    p.t([Bf("cwem"), Bf("sm_ind")], [Bf("pb4")],
                        lambda e, ex=ex: e.matmul(
                            pb[4][:, 0:TT],
                            sml["ind"][:, ex * 128:(ex + 1) * 128], cwem,
                            start=True, stop=True))
                    p.v([Bf("pb4")], [Bf("cwb_sb")],
                        lambda e: e.tensor_copy(cwb_sb, pb[4][:, 0:TT]))
                for pr in range(8):
                    u = eu + pr
                    want_unit(u)
                    rb, rbuf = unit_buf(u)
                    for slot in range(4):
                        mm_group(
                            slot, pb[slot][:, 0:TT],
                            (lambda kk, rb=rb, slot=slot:
                             rb[:, slot * 1024 + kk * 128:
                                slot * 1024 + (kk + 1) * 128], [rbuf]),
                            [Bf("ln_t")],
                            lambda kk: chunk(ln_t, kk), KC)
                    boff = ex * 16
                    p.s([Bf("pb0"), Bf("sm_b1c")], [Bf("hh1")],
                        lambda e, pr=pr, boff=boff: e.activation(
                            out=hh1[:, 0:TT], in_=pb[0][:, 0:TT],
                            func=AF.Identity,
                            bias=sml["b1c"][:, boff + 2 * pr:
                                            boff + 2 * pr + 1]))
                    p.v([Bf("pb2"), Bf("hh1"), Bf("sm_b3c")], [Bf("hg_s")],
                        lambda e, pr=pr, boff=boff: e.scalar_tensor_tensor(
                            hg_s, pb[2][:, 0:TT],
                            sml["b3c"][:, boff + 2 * pr:boff + 2 * pr + 1],
                            hh1[:, 0:TT], OP.add, OP.mult))
                    p.s([Bf("pb1"), Bf("sm_b1c")], [Bf("hh2")],
                        lambda e, pr=pr, boff=boff: e.activation(
                            out=hh2[:, 0:TT], in_=pb[1][:, 0:TT],
                            func=AF.Identity,
                            bias=sml["b1c"][:, boff + 2 * pr + 1:
                                            boff + 2 * pr + 2]))
                    p.v([Bf("pb3"), Bf("hh2"), Bf("sm_b3c")], [Bf("hl_s")],
                        lambda e, pr=pr, boff=boff: e.scalar_tensor_tensor(
                            hl_s, pb[3][:, 0:TT],
                            sml["b3c"][:, boff + 2 * pr + 1:
                                       boff + 2 * pr + 2],
                            hh2[:, 0:TT], OP.add, OP.mult))
                    p.v([Bf("hg_s")], [Bf("hg_s")],
                        lambda e: e.tensor_scalar(
                            hg_s, hg_s, LIMIT, None, OP.min))
                    p.s([Bf("hg_s")], [Bf("sg_s")],
                        lambda e: e.activation(
                            out=sg_s, in_=hg_s, func=AF.Sigmoid,
                            scale=ALPHA))
                    p.v([Bf("hg_s"), Bf("sg_s")], [Bf("hg_s")],
                        lambda e: e.tensor_tensor(hg_s, hg_s, sg_s,
                                                  OP.mult))
                    p.v([Bf("hl_s")], [Bf("hl_s")],
                        lambda e: e.tensor_scalar(
                            hl_s, hl_s, -LIMIT, LIMIT, OP.max, OP.min))
                    p.v([Bf("hl_s"), Bf("hg_s")], [Bf("a_t")],
                        lambda e, pr=pr: e.scalar_tensor_tensor(
                            chunk(a_t, pr), hl_s, 1.0, hg_s,
                            OP.add, OP.mult))
                    if routed:
                        p.v([Bf("a_t"), Bf("cwb_sb")], [Bf("a_t")],
                            lambda e, pr=pr: e.tensor_tensor(
                                chunk(a_t, pr), chunk(a_t, pr), cwb_sb,
                                OP.mult))
                for dc in range(KC):
                    u = eu + 8 + dc // 4
                    want_unit(u)
                    rb, rbuf = unit_buf(u)
                    sub = dc % 4
                    bi = dc % 2
                    bank = Bf(f"pb{bi}")

                    def fn(e, rb=rb, sub=sub, bi=bi):
                        inst = None
                        for kk in range(KC):
                            inst = e.matmul(
                                pb[bi][:, 0:TT],
                                rb[:, sub * 1024 + kk * 128:
                                   sub * 1024 + (kk + 1) * 128],
                                chunk(a_t, kk),
                                start=(kk == 0),
                                stop=(kk == KC - 1))
                        return inst
                    p.emit("t", "t", 1, [Bf("a_t"), rbuf], [bank], fn,
                           skip_waw=True)
                    if routed:
                        if ex == 0:
                            p.v([bank, Bf("cwb_sb"), Bf("sm_b2c")],
                                [Bf("acc")],
                                lambda e, dc=dc, bi=bi, ex=ex:
                                e.scalar_tensor_tensor(
                                    chunk(acc, dc), cwb_sb,
                                    sml["b2c"][:, ex * 8 + dc:
                                               ex * 8 + dc + 1],
                                    pb[bi][:, 0:TT], OP.mult, OP.add))
                        else:
                            p.v([bank, Bf("cwb_sb"), Bf("sm_b2c")],
                                [Bf("hh1")],
                                lambda e, dc=dc, bi=bi, ex=ex:
                                e.scalar_tensor_tensor(
                                    hh1[:, 0:TT], cwb_sb,
                                    sml["b2c"][:, ex * 8 + dc:
                                               ex * 8 + dc + 1],
                                    pb[bi][:, 0:TT], OP.mult, OP.add))
                            p.v([Bf("hh1"), Bf("acc")], [Bf("acc")],
                                lambda e, dc=dc: e.tensor_tensor(
                                    chunk(acc, dc), chunk(acc, dc),
                                    hh1[:, 0:TT], OP.add))
                    else:
                        p.v([bank, Bf("acc"), Bf("sm_sb2c")], [Bf("acc")],
                            lambda e, dc=dc, bi=bi: e.scalar_tensor_tensor(
                                chunk(acc, dc), pb[bi][:, 0:TT],
                                sml["sb2c"][:, dc:dc + 1],
                                chunk(acc, dc), OP.add, OP.add))

            if "acc" in DEBUG_TAPS and g == DEBUG_TILE[0]:
                dbg3 = nc.dram_tensor("dbg_acc", [128, W], F32,
                                      kind="ExternalOutput").ap()
                p.dma("g", "qd2", [Bf("acc")], [Bf("dbg3")],
                      lambda e: e.dma_start(out=dbg3, in_=acc))

            for kc in range(KC):
                p.v([Bf("x_t"), Bf("acc")], [Bf("acc")],
                    lambda e, kc=kc: e.tensor_tensor(
                        chunk(acc, kc), chunk(x_t, kc), chunk(acc, kc),
                        OP.add))
            p.dma("g", "qy", [Bf("acc")], [Bf("y_out")],
                  lambda e, t0=t0: e.dma_start(
                        out=y_h.rearrange("(k p) t -> p k t",
                                          k=KC)[:, :, t0:t0 + TT],
                        in_=acc.rearrange("p (k t) -> p k t", k=KC)))

        final = p.cnt["qy"]
        p.raw("g", lambda e: e.wait_ge(p.sems["qy"], final))
        for i in range(6):
            fv = p.cnt[f"qd{i}"]
            if fv:
                p.raw("g", lambda e, i=i, fv=fv: e.wait_ge(p.sems[f"qd{i}"], fv))

        @block.tensor
        def _(eng):
            for f in p.streams["t"]:
                f(eng)

        @block.vector
        def _(eng):
            for f in p.streams["v"]:
                f(eng)

        @block.scalar
        def _(eng):
            for f in p.streams["s"]:
                f(eng)

        @block.sync
        def _(eng):
            for f in p.streams["y"]:
                f(eng)

        @block.gpsimd
        def _(eng):
            for f in p.streams["g"]:
                f(eng)

    return nc


def kernel(**inputs):
    T = inputs["L0_attn_x"].shape[1]
    x = np.asarray(inputs["L0_attn_x"], np.float32)
    nb = x.shape[0]
    wpack = build_wpack({k: np.asarray(v, np.float32)
                         for k, v in inputs.items()})
    smalls = build_smalls({k: np.asarray(v, np.float32)
                           for k, v in inputs.items()})
    nc = build_program(T)
    in_maps = []
    for b in range(nb):
        m = {"x": np.ascontiguousarray(x[b].T), "wpack": wpack}
        m.update(smalls)
        in_maps.append(m)
    res = run_bass_kernel_spmd(nc, in_maps, core_ids=list(range(nb)))
    out = np.stack([np.asarray(r["y"]).T for r in res.results])
    return np.ascontiguousarray(out.astype(np.float32))


# version 03
# speedup vs baseline: 1.1819x; 1.1819x over previous
import numpy as np
from contextlib import ExitStack

import concourse.bass as bass
import concourse.mybir as mybir
from concourse.bass_utils import run_bass_kernel_spmd

F32 = mybir.dt.float32
F16 = mybir.dt.float16
AF = mybir.ActivationFunctionType
OP = mybir.AluOpType
AX = mybir.AxisListType

B, C, H, S = 8, 1024, 16, 64
KC = C // 128
E = 8
F2 = 2048
LN_EPS = 1e-5
ALPHA, LIMIT = 1.702, 7.0
NEG_BIG = -1.0e30
LS = 2048.0

TT = 512
RING = 2
UNITS = 98


def _pack_lhsT(WT):
    K, M = WT.shape
    L = WT.reshape(K // 128, 128, M // 128, 128).transpose(2, 1, 0, 3)
    return np.ascontiguousarray(L)


def _unit4(L4):
    n = L4.shape[0]
    X = L4.shape[2]
    return np.ascontiguousarray(L4.transpose(1, 0, 2).reshape(128, n * X))


def _perm_f():
    ge = np.arange(0, F2, 2)
    le = np.arange(1, F2, 2)
    out = []
    for i in range(8):
        out.append(ge[128 * i:128 * (i + 1)])
        out.append(le[128 * i:128 * (i + 1)])
    return np.concatenate(out)


def _colpack(v):
    return np.ascontiguousarray(np.asarray(v, np.float32).reshape(KC, 128).T)


def build_wpack(ins):
    perm = _perm_f()
    units = []
    for nm in ("Wk", "Wv", "Wr", "Wo"):
        L = _pack_lhsT(np.ascontiguousarray(ins["L0_attn_" + nm].T))
        units.append(_unit4(L[0:4].reshape(4, 128, KC * 128)))
        units.append(_unit4(L[4:8].reshape(4, 128, KC * 128)))
    for e in range(9):
        if e < E:
            w1 = ins["L0_ffn_w1"][e][perm]
            w3 = ins["L0_ffn_w3"][e][perm]
            w2 = ins["L0_ffn_w2"][e]
        else:
            w1 = ins["L0_ffn_sw1"][perm]
            w3 = ins["L0_ffn_sw3"][perm]
            w2 = ins["L0_ffn_sw2"]
        L1 = _pack_lhsT(np.ascontiguousarray(w1.T)).reshape(16, 128, KC * 128)
        L3 = _pack_lhsT(np.ascontiguousarray(w3.T)).reshape(16, 128, KC * 128)
        for i in range(8):
            quad = np.stack([L1[2 * i], L1[2 * i + 1], L3[2 * i], L3[2 * i + 1]])
            units.append(_unit4(quad))
        L2 = _pack_lhsT(np.ascontiguousarray(w2.T)).reshape(8, 128, KC * 128)
        units.append(_unit4(L2[0:4]))
        units.append(_unit4(L2[4:8]))
    return np.ascontiguousarray(np.stack(units))


def build_smalls(ins):
    perm = _perm_f()
    b1c, b3c = [], []
    for e in range(E):
        b1c.append(ins["L0_ffn_b1"][e][perm])
        b3c.append(ins["L0_ffn_b3"][e][perm])
    b1c.append(ins["L0_ffn_sb1"][perm])
    b3c.append(ins["L0_ffn_sb3"][perm])
    b1c = np.concatenate(
        [np.asarray(b, np.float32).reshape(16, 128).T for b in b1c], axis=1)
    b3c = np.concatenate(
        [np.asarray(b, np.float32).reshape(16, 128).T for b in b3c], axis=1)
    b2c = np.concatenate(
        [np.asarray(ins["L0_ffn_b2"][e], np.float32).reshape(8, 128).T
         for e in range(E)], axis=1)
    gw = np.ascontiguousarray(
        np.asarray(ins["L0_ffn_gate_w"], np.float32).T
        .reshape(KC, 128, E).transpose(1, 0, 2).reshape(128, KC * E))
    ind = np.zeros((E, E * 128), np.float32)
    for e in range(E):
        ind[e, e * 128:(e + 1) * 128] = 1.0
    return {
        "mixk": _colpack(ins["L0_attn_time_mix_k"].reshape(-1)),
        "mixv": _colpack(ins["L0_attn_time_mix_v"].reshape(-1)),
        "mixr": _colpack(ins["L0_attn_time_mix_r"].reshape(-1)),
        "decr": _colpack(ins["L0_attn_time_decay"].reshape(-1)),
        "tfc": _colpack(ins["L0_attn_time_first"].reshape(-1)),
        "lng": _colpack(ins["L0_attn_ln_g"]),
        "lnb": _colpack(ins["L0_attn_ln_b"]),
        "gw": gw,
        "gb": np.asarray(ins["L0_ffn_gate_b"], np.float32)[None, :],
        "b1c": np.ascontiguousarray(b1c),
        "b3c": np.ascontiguousarray(b3c),
        "b2c": np.ascontiguousarray(b2c),
        "sb2c": _colpack(ins["L0_ffn_sb2"]),
        "ind": ind,
        "eye8": np.eye(8, dtype=np.float32),
        "eye128": np.eye(128, dtype=np.float32),
    }


class Buf:
    __slots__ = ("w", "r")

    def __init__(self):
        self.w = None
        self.r = {}


class Prog:

    def __init__(self, nc, sem_names):
        self.nc = nc
        self.streams = {"t": [], "v": [], "s": [], "g": [], "y": []}
        self.cnt = {n: 0 for n in sem_names}
        self.wm = {e: {} for e in self.streams}
        self.sems = {}

    def emit(self, stream, sem, inc, reads, writes, fn, skip_waw=False):
        waits = {}

        def need(s2, v2):
            if v2 and v2 > waits.get(s2, 0):
                waits[s2] = v2

        for b in reads:
            if b.w:
                need(*b.w)
        for b in writes:
            if b.w and not skip_waw:
                need(*b.w)
            for s2, v2 in b.r.items():
                need(s2, v2)
        wm = self.wm[stream]
        waits = {s2: v2 for s2, v2 in waits.items() if v2 > wm.get(s2, 0)}
        for s2, v2 in waits.items():
            wm[s2] = v2
        self.cnt[sem] += inc
        mark = self.cnt[sem]
        wl = sorted(waits.items())
        sems = self.sems

        def closure(eng, wl=wl, fn=fn, sem=sem, inc=inc):
            for s2, v2 in wl:
                eng.wait_ge(sems[s2], v2)
            inst = fn(eng)
            inst.then_inc(sems[sem], inc)

        self.streams[stream].append(closure)
        for b in writes:
            b.w = (sem, mark)
            b.r = {}
        for b in reads:
            if b.r.get(sem, 0) < mark:
                b.r[sem] = mark
        return mark

    def t(self, reads, writes, fn):
        return self.emit("t", "t", 1, reads, writes, fn, skip_waw=True)

    def v(self, reads, writes, fn):
        return self.emit("v", "v", 1, reads, writes, fn)

    def s(self, reads, writes, fn):
        return self.emit("s", "s", 1, reads, writes, fn)

    def dma(self, stream, sem, reads, writes, fn):
        return self.emit(stream, sem, 16, reads, writes, fn)

    def raw(self, stream, fn):
        self.streams[stream].append(fn)


DEBUG_TAPS = []
DEBUG_TILE = [0]


def build_program(T):
    NT = T // TT
    W = KC * TT
    nc = bass.Bass("TRN2", target_bir_lowering=False)

    x_h = nc.dram_tensor("x", [C, T], F32, kind="ExternalInput").ap()
    y_h = nc.dram_tensor("y", [C, T], F32, kind="ExternalOutput").ap()
    wpack = nc.dram_tensor("wpack", [UNITS, 128, 4096], F32,
                           kind="ExternalInput").ap()
    whbm = nc.dram_tensor("whbm", [UNITS, 128, 8192], F16,
                          kind="Internal").ap()
    sm_shapes = {
        "mixk": [128, 8], "mixv": [128, 8], "mixr": [128, 8],
        "decr": [128, 8], "tfc": [128, 8], "lng": [128, 8], "lnb": [128, 8],
        "gw": [128, KC * E], "gb": [1, 8], "b1c": [128, 144],
        "b3c": [128, 144], "b2c": [128, 64], "sb2c": [128, 8],
        "ind": [8, 1024], "eye8": [8, 8], "eye128": [128, 128],
    }
    sm_h = {nm: nc.dram_tensor(nm, shp, F32, kind="ExternalInput").ap()
            for nm, shp in sm_shapes.items()}

    def sb(name, shape, dt=F32):
        return nc.alloc_sbuf_tensor(name, shape, dt).ap()

    x_t = sb("x_t", [128, W])
    ln_t = sb("ln_t", [128, W])
    sh_t = sb("sh_t", [128, W])
    k_t = sb("k_t", [128, W])
    kv_t = sb("kv_t", [128, W])
    r_t = sb("r_t", [128, W])
    pAh = sb("pAh", [128, W], F16)
    pAl = sb("pAl", [128, W], F16)
    pLh = sb("pLh", [128, W], F16)
    pLl = sb("pLl", [128, W], F16)
    pGh = sb("pGh", [128, W], F16)
    pGl = sb("pGl", [128, W], F16)
    ring = [sb(f"ring{i}", [128, 8192], F16) for i in range(RING)]
    c32a = sb("c32a", [128, TT])
    c32b = sb("c32b", [128, TT])
    hh1 = sb("hh1", [128, TT])
    hh2 = sb("hh2", [128, TT])
    hg_s = sb("hg_s", [128, TT])
    hl_s = sb("hl_s", [128, TT])
    cwb_sb = sb("cwb_sb", [128, TT])
    g_sb = sb("g_sb", [8, TT])
    gt_s = sb("gt_s", [128, (TT // 128) * 8])
    cwt_s = sb("cwt_s", [128, (TT // 128) * 8])
    cwem = sb("cwem", [8, TT])
    grow = sb("grow", [1, 3 * TT])
    sml = {nm: sb("s_" + nm, shp) for nm, shp in sm_shapes.items()}
    dec_c_h = nc.alloc_sbuf_tensor("dec_c", [128, 8], F32)
    dec_c = dec_c_h.ap()
    gbb = sb("gbb", [128, 8])
    car_ln = [sb("car_ln0", [128, 8]), sb("car_ln1", [128, 8])]
    car_sc = sb("car_sc", [128, 8])
    ones_c = sb("ones_c", [128, 1])
    ones_l = sb("ones_l", [1, 128])
    eps_c = sb("eps_c", [1, 1])

    pb = [nc.alloc_psum_tensor(f"pb{i}", [128, 512], F32).ap()
          for i in range(8)]

    SEMS = (["t", "v", "s", "qx", "qy", "qg", "qh", "ql"]
            + [f"qs{i}" for i in range(RING)])
    p = Prog(nc, SEMS)
    bufs = {}
    _ALIAS = {"s_t": "sh_t", "acc": "sh_t"}

    def Bf(name):
        name = _ALIAS.get(name, name)
        if name not in bufs:
            bufs[name] = Buf()
        return bufs[name]

    s_t = sh_t
    acc = sh_t

    with ExitStack() as ctx:
        for s2 in SEMS:
            p.sems[s2] = ctx.enter_context(nc.semaphore("sem_" + s2))
        block = ctx.enter_context(nc.Block())

        for nm in sm_shapes:
            p.dma("g", "qg", [], [Bf("sm_" + nm)],
                  lambda e, nm=nm: e.dma_start(out=sml[nm], in_=sm_h[nm]))
        for nm in sm_shapes:
            bufs["sm_" + nm].w = ("qg", p.cnt["qg"])
        p.v([], [Bf("ones")], lambda e: e.memset(ones_c, 1.0))
        p.v([], [Bf("ones")], lambda e: e.memset(ones_l, 1.0))
        p.v([], [Bf("eps")], lambda e: e.memset(eps_c, LN_EPS))
        p.v([], [Bf("car_ln0")], lambda e: e.memset(car_ln[0], 0.0))
        p.v([], [Bf("car_ln1")], lambda e: e.memset(car_ln[1], 0.0))
        p.v([], [Bf("car_sc")], lambda e: e.memset(car_sc, 0.0))
        p.s([Bf("sm_decr")], [Bf("dec_c")],
            lambda e: e.activation(out=dec_c, in_=sml["decr"], func=AF.Exp))
        p.t([Bf("ones"), Bf("sm_gb")], [Bf("pb7")],
            lambda e: e.matmul(pb[7][:, 0:8], ones_l, sml["gb"],
                               start=True, stop=True))
        p.v([Bf("pb7")], [Bf("gbb")],
            lambda e: e.tensor_copy(gbb, pb[7][:, 0:8]))

        for u in range(UNITS):
            ev = u % 2 == 0
            stg, stg_b = (x_t, Bf("x_t")) if ev else (ln_t, Bf("ln_t"))
            hi, hi_b = (pAh, Bf("pAh")) if ev else (pLh, Bf("pLh"))
            lo, lo_b = (pAl, Bf("pAl")) if ev else (pLl, Bf("pLl"))
            d, d_b = (k_t, Bf("k_t")) if ev else (kv_t, Bf("kv_t"))
            p.dma("y", f"qs{u % RING}", [], [stg_b],
                  lambda e, u=u, stg=stg: e.dma_start(out=stg, in_=wpack[u]))
            p.s([stg_b], [hi_b],
                lambda e, stg=stg, hi=hi: e.activation(
                    out=hi, in_=stg, func=AF.Copy))
            p.v([stg_b, hi_b], [d_b],
                lambda e, stg=stg, hi=hi, d=d: e.tensor_tensor(
                    d, stg, hi, OP.subtract))
            p.s([d_b], [lo_b],
                lambda e, d=d, lo=lo: e.activation(
                    out=lo, in_=d, func=AF.Copy, scale=LS))
            p.dma("g", "qh", [hi_b], [Bf("whbm")],
                  lambda e, u=u, hi=hi: e.dma_start(
                      out=whbm[u][:, 0:4096], in_=hi))
            p.dma("g", "ql", [lo_b], [Bf("whbm")],
                  lambda e, u=u, lo=lo: e.dma_start(
                      out=whbm[u][:, 4096:8192], in_=lo))
        wrdy_h = Buf()
        wrdy_h.w = ("qh", p.cnt["qh"])
        wrdy_l = Buf()
        wrdy_l.w = ("ql", p.cnt["ql"])
        wrdy = [wrdy_h, wrdy_l]

        unit_state = {"emitted": 0}

        def unit_buf(u):
            return ring[u % RING], Bf(f"ring{u % RING}")

        def want_unit(u):
            hi = min(u + RING, NT * UNITS)
            while unit_state["emitted"] < hi:
                uu = unit_state["emitted"]
                rb, rbuf = unit_buf(uu)
                p.dma("y", f"qs{uu % RING}", wrdy, [rbuf],
                      lambda e, uu=uu, rb=rb:
                      e.dma_start(out=rb, in_=whbm[uu % UNITS]))
                unit_state["emitted"] += 1

        def chunk(t, kc, n=TT):
            return t[:, kc * n:(kc + 1) * n]

        GR = {}
        for i, nm in enumerate(("mean", "msq", "rstd")):
            GR[nm] = grow[0:1, i * TT:(i + 1) * TT]
        GR["var"] = GR["msq"]
        GR["negm"] = GR["mean"]
        grow_b = Bf("grow")

        def bcols(t, col):
            return t[:, col::TT]

        def mm3(a_i, b_i, rb, rbuf, base, xh, xl, xb):
            lhs_hi = [rb[:, base + kk * 128: base + (kk + 1) * 128]
                      for kk in range(KC)]
            lhs_lo = [rb[:, 4096 + base + kk * 128:
                         4096 + base + (kk + 1) * 128]
                      for kk in range(KC)]

            def fa(e):
                for kk in range(KC):
                    inst = e.matmul(pb[a_i][:, 0:TT], lhs_hi[kk],
                                    chunk(xh, kk),
                                    start=(kk == 0), stop=(kk == KC - 1))
                return inst

            def fb(e):
                for kk in range(KC):
                    e.matmul(pb[b_i][:, 0:TT], lhs_hi[kk], chunk(xl, kk),
                             start=(kk == 0), stop=False)
                for kk in range(KC):
                    inst = e.matmul(pb[b_i][:, 0:TT], lhs_lo[kk],
                                    chunk(xh, kk),
                                    start=False, stop=(kk == KC - 1))
                return inst
            p.t([rbuf] + xb, [Bf(f"pb{a_i}")], fa)
            p.t([rbuf] + xb, [Bf(f"pb{b_i}")], fb)

        def evac3(a_i, b_i, dst, dst_b, bias=None, bias_b=None):
            if bias is None:
                p.s([Bf(f"pb{a_i}")], [dst_b],
                    lambda e: e.activation(out=dst, in_=pb[a_i][:, 0:TT],
                                           func=AF.Copy))
            else:
                p.s([Bf(f"pb{a_i}"), bias_b], [dst_b],
                    lambda e: e.activation(out=dst, in_=pb[a_i][:, 0:TT],
                                           func=AF.Identity, bias=bias))
            p.v([Bf(f"pb{b_i}"), dst_b], [dst_b],
                lambda e: e.scalar_tensor_tensor(
                    dst, pb[b_i][:, 0:TT], 1.0 / LS, dst, OP.mult, OP.add))

        def split_chunk(src, src_b, hi, lo, hl_b, scr, scr_b):
            p.s([src_b], [hl_b],
                lambda e: e.activation(out=hi, in_=src, func=AF.Copy))
            p.v([src_b, hl_b], [scr_b],
                lambda e: e.tensor_tensor(scr, src, hi, OP.subtract))
            p.s([scr_b], [hl_b],
                lambda e: e.activation(out=lo, in_=scr, func=AF.Copy,
                                       scale=LS))

        def layer_norm_stats(src, src_b):
            for kc in range(KC):
                scr = c32a if kc % 2 == 0 else c32b
                scr_b = Bf("c32a") if kc % 2 == 0 else Bf("c32b")
                p.s([src_b], [scr_b],
                    lambda e, kc=kc, scr=scr: e.activation(
                        out=scr, in_=chunk(src, kc), func=AF.Square))
                p.t([src_b, Bf("ones")], [Bf("pb6")],
                    lambda e, kc=kc: e.matmul(
                        pb[6][0:1, 0:TT], ones_c, chunk(src, kc),
                        start=(kc == 0), stop=(kc == KC - 1)))
                p.t([scr_b, Bf("ones")], [Bf("pb6")],
                    lambda e, kc=kc, scr=scr: e.matmul(
                        pb[6][32:33, 0:TT], ones_c, scr,
                        start=(kc == 0), stop=(kc == KC - 1)))
            st6 = Bf("pb6")
            p.v([st6], [grow_b], lambda e: e.tensor_scalar_mul(
                GR["mean"], pb[6][0:1, 0:TT], 1.0 / C))
            p.v([st6, grow_b], [grow_b], lambda e: e.tensor_scalar_mul(
                GR["msq"], pb[6][32:33, 0:TT], 1.0 / C))
            p.v([grow_b], [grow_b], lambda e: e.tensor_tensor(
                GR["rstd"], GR["mean"], GR["mean"], OP.mult))
            p.v([grow_b], [grow_b], lambda e: e.tensor_tensor(
                GR["var"], GR["msq"], GR["rstd"], OP.subtract))
            p.s([grow_b, Bf("eps")], [grow_b], lambda e: e.activation(
                out=GR["rstd"], in_=GR["var"], func=AF.Sqrt, bias=eps_c))
            p.v([grow_b], [grow_b],
                lambda e: e.reciprocal(GR["rstd"], GR["rstd"]))
            p.v([grow_b], [grow_b], lambda e: e.scalar_tensor_tensor(
                GR["negm"], GR["mean"], -1.0, GR["rstd"], OP.mult, OP.mult))
            p.t([grow_b, Bf("ones")], [Bf("pb4")],
                lambda e: e.matmul(pb[4][:, 0:TT], ones_l, GR["rstd"],
                                   start=True, stop=True))
            p.t([grow_b, Bf("ones")], [Bf("pb5")],
                lambda e: e.matmul(pb[5][:, 0:TT], ones_l, GR["negm"],
                                   start=True, stop=True))

        def ln_chunk(src, src_b, kc, dst, dst_b, affine):
            p.v([src_b, Bf("pb4")], [dst_b],
                lambda e, kc=kc, dst=dst: e.tensor_tensor(
                    dst, chunk(src, kc), pb[4][:, 0:TT], OP.mult))
            p.v([dst_b, Bf("pb5")], [dst_b],
                lambda e, dst=dst: e.tensor_tensor(
                    dst, dst, pb[5][:, 0:TT], OP.add))
            if affine:
                p.v([dst_b, Bf("sm_lng"), Bf("sm_lnb")], [dst_b],
                    lambda e, kc=kc, dst=dst: e.tensor_scalar(
                        dst, dst, sml["lng"][:, kc:kc + 1],
                        sml["lnb"][:, kc:kc + 1], OP.mult, OP.add))

        for g in range(NT):
            t0 = g * TT
            u0 = g * UNITS
            carA, carB = car_ln[g % 2], car_ln[(g + 1) % 2]
            carA_b, carB_b = Bf(f"car_ln{g % 2}"), Bf(f"car_ln{(g + 1) % 2}")

            p.dma("g", "qx", [], [Bf("x_t")],
                  lambda e, t0=t0: e.dma_start(
                      out=x_t.rearrange("p (k t) -> p k t", k=KC),
                      in_=x_h.rearrange("(k p) t -> p k t",
                                        k=KC)[:, :, t0:t0 + TT]))

            layer_norm_stats(x_t, Bf("x_t"))
            for kc in range(KC):
                ln_chunk(x_t, Bf("x_t"), kc, chunk(ln_t, kc), Bf("ln_t"),
                         affine=False)
            p.v([Bf("ln_t")], [carB_b],
                lambda e, carB=carB: e.tensor_copy(carB, bcols(ln_t, TT - 1)))
            p.v([Bf("ln_t")], [Bf("sh_t")],
                lambda e: e.tensor_copy(sh_t[:, 1:W], ln_t[:, 0:W - 1]))
            p.v([carA_b, Bf("sh_t")], [Bf("sh_t")],
                lambda e, carA=carA: e.tensor_copy(bcols(sh_t, 0), carA))

            for pi, (mixnm, kind) in enumerate(
                    (("mixk", "k"), ("mixv", "v"), ("mixr", "r"))):
                for kc in range(KC):
                    scr = c32a if kc % 2 == 0 else c32b
                    scr_b = Bf("c32a") if kc % 2 == 0 else Bf("c32b")
                    p.v([Bf("ln_t"), Bf("sh_t")], [scr_b],
                        lambda e, kc=kc, scr=scr: e.tensor_tensor(
                            scr, chunk(ln_t, kc), chunk(sh_t, kc),
                            OP.subtract))
                    p.v([scr_b, Bf("sh_t"), Bf("sm_" + mixnm)], [scr_b],
                        lambda e, kc=kc, scr=scr, mixnm=mixnm:
                        e.scalar_tensor_tensor(
                            scr, scr, sml[mixnm][:, kc:kc + 1],
                            chunk(sh_t, kc), OP.mult, OP.add))
                    split_chunk(scr, scr_b, chunk(pAh, kc), chunk(pAl, kc),
                                Bf("pA"), hh1, Bf("hh1"))
                for dc in range(KC):
                    u = u0 + 2 * pi + dc // 4
                    want_unit(u)
                    rb, rbuf = unit_buf(u)
                    base = (dc % 4) * 1024
                    a_i, b_i = (0, 1) if dc % 2 == 0 else (2, 3)
                    mm3(a_i, b_i, rb, rbuf, base, pAh, pAl, [Bf("pA")])
                    if kind == "k":
                        evac3(a_i, b_i, chunk(k_t, dc), Bf("k_t"))
                    elif kind == "v":
                        evac3(a_i, b_i, hh2, Bf("hh2"))
                        p.v([Bf("hh2"), Bf("k_t")], [Bf("kv_t")],
                            lambda e, dc=dc: e.tensor_tensor(
                                chunk(kv_t, dc), chunk(k_t, dc), hh2,
                                OP.mult))
                    else:
                        evac3(a_i, b_i, chunk(r_t, dc), Bf("r_t"))

            for kc in range(KC):
                dec_b = bass.AP(dec_c_h, kc, [[8, 128], [0, TT]])
                p.v([Bf("kv_t"), Bf("dec_c"), Bf("car_sc")], [Bf("s_t")],
                    lambda e, kc=kc, dec_b=dec_b: e.tensor_tensor_scan(
                        chunk(s_t, kc), dec_b, chunk(kv_t, kc),
                        car_sc[:, kc:kc + 1], OP.mult, OP.add))
            p.v([Bf("s_t")], [Bf("car_sc")],
                lambda e: e.tensor_copy(car_sc, bcols(s_t, TT - 1)))

            for kc in range(KC):
                p.v([Bf("kv_t"), Bf("s_t"), Bf("sm_tfc")], [Bf("s_t")],
                    lambda e, kc=kc: e.scalar_tensor_tensor(
                        chunk(s_t, kc), chunk(kv_t, kc),
                        sml["tfc"][:, kc:kc + 1], chunk(s_t, kc),
                        OP.mult, OP.add))
            for kc in range(KC):
                p.v([Bf("s_t"), Bf("r_t")], [Bf("kv_t")],
                    lambda e, kc=kc: e.tensor_tensor(
                        chunk(kv_t, kc), chunk(r_t, kc), chunk(s_t, kc),
                        OP.mult))

            layer_norm_stats(kv_t, Bf("kv_t"))
            for kc in range(KC):
                scr = c32a if kc % 2 == 0 else c32b
                scr_b = Bf("c32a") if kc % 2 == 0 else Bf("c32b")
                ln_chunk(kv_t, Bf("kv_t"), kc, scr, scr_b, affine=True)
                split_chunk(scr, scr_b, chunk(pAh, kc), chunk(pAl, kc),
                            Bf("pA"), hh1, Bf("hh1"))

            for dc in range(KC):
                u = u0 + 6 + dc // 4
                want_unit(u)
                rb, rbuf = unit_buf(u)
                base = (dc % 4) * 1024
                a_i, b_i = (0, 1) if dc % 2 == 0 else (2, 3)
                mm3(a_i, b_i, rb, rbuf, base, pAh, pAl, [Bf("pA")])
                evac3(a_i, b_i, hh2, Bf("hh2"))
                p.v([Bf("hh2"), Bf("x_t")], [Bf("x_t")],
                    lambda e, dc=dc: e.tensor_tensor(
                        chunk(x_t, dc), chunk(x_t, dc), hh2, OP.add))

            layer_norm_stats(x_t, Bf("x_t"))
            for kc in range(KC):
                scr = c32a if kc % 2 == 0 else c32b
                scr_b = Bf("c32a") if kc % 2 == 0 else Bf("c32b")
                ln_chunk(x_t, Bf("x_t"), kc, scr, scr_b, affine=False)
                p.t([scr_b, Bf("sm_gw")], [Bf("pb7")],
                    lambda e, kc=kc, scr=scr: e.matmul(
                        pb[7][0:8, 0:TT], sml["gw"][:, kc * 8:(kc + 1) * 8],
                        scr, start=(kc == 0), stop=(kc == KC - 1)))
                split_chunk(scr, scr_b, chunk(pLh, kc), chunk(pLl, kc),
                            Bf("pL"), hh1, Bf("hh1"))

            p.v([Bf("pb7")], [Bf("g_sb")],
                lambda e: e.tensor_copy(g_sb, pb[7][0:8, 0:TT]))
            nsub = TT // 128
            for sub in range(nsub):
                p.t([Bf("g_sb"), Bf("sm_eye8")], [Bf("pb6")],
                    lambda e, sub=sub: e.transpose(
                        pb[6][:, 0:8], g_sb[:, sub * 128:(sub + 1) * 128],
                        sml["eye8"]))
                p.v([Bf("pb6")], [Bf("gt_s")],
                    lambda e, sub=sub: e.tensor_copy(
                        gt_s[:, sub * 8:(sub + 1) * 8], pb[6][:, 0:8]))
            for sub in range(nsub):
                gts = gt_s[:, sub * 8:(sub + 1) * 8]
                c_m = cwb_sb[:, 0:1]
                c_s = cwb_sb[:, 1:2]
                et = hh1[:, 0:8]
                bt = hh2[:, 0:8]
                p.v([Bf("gt_s")], [Bf("cwb_sb")],
                    lambda e, gts=gts, c_m=c_m: e.tensor_reduce(
                        c_m, gts, AX.X, OP.max))
                p.v([Bf("cwb_sb")], [Bf("cwb_sb")],
                    lambda e, c_m=c_m: e.tensor_scalar_mul(c_m, c_m, -1.0))
                p.s([Bf("gt_s"), Bf("cwb_sb")], [Bf("hh1")],
                    lambda e, gts=gts, et=et, c_m=c_m: e.activation(
                        out=et, in_=gts, func=AF.Exp, bias=c_m))
                p.v([Bf("hh1")], [Bf("cwb_sb")],
                    lambda e, et=et, c_s=c_s: e.tensor_reduce(
                        c_s, et, AX.X, OP.add))
                p.v([Bf("cwb_sb")], [Bf("cwb_sb")],
                    lambda e, c_s=c_s: e.reciprocal(c_s, c_s))
                p.v([Bf("hh1"), Bf("cwb_sb")], [Bf("hh1")],
                    lambda e, et=et, c_s=c_s: e.tensor_scalar_mul(
                        et, et, c_s))
                p.v([Bf("hh1"), Bf("gbb")], [Bf("hh2")],
                    lambda e, et=et, bt=bt: e.tensor_tensor(
                        bt, et, gbb, OP.add))
                p.v([Bf("hh2")], [Bf("cwb_sb")],
                    lambda e, bt=bt, c_m=c_m: e.tensor_reduce(
                        c_m, bt, AX.X, OP.max))
                p.v([Bf("hh2"), Bf("cwb_sb")], [Bf("hl_s")],
                    lambda e, bt=bt, c_m=c_m: e.tensor_scalar(
                        hl_s[:, 0:8], bt, c_m, None, OP.is_equal))
                p.v([Bf("hh2"), Bf("hl_s")], [Bf("hh2")],
                    lambda e, bt=bt: e.scalar_tensor_tensor(
                        bt, hl_s[:, 0:8], NEG_BIG, bt, OP.mult, OP.add))
                p.v([Bf("hh2")], [Bf("cwb_sb")],
                    lambda e, bt=bt, c_s=c_s: e.tensor_reduce(
                        c_s, bt, AX.X, OP.max))
                p.v([Bf("hh2"), Bf("cwb_sb")], [Bf("hg_s")],
                    lambda e, bt=bt, c_s=c_s: e.tensor_scalar(
                        hg_s[:, 0:8], bt, c_s, None, OP.is_equal))
                p.v([Bf("hl_s"), Bf("hg_s")], [Bf("hl_s")],
                    lambda e: e.tensor_tensor(
                        hl_s[:, 0:8], hl_s[:, 0:8], hg_s[:, 0:8], OP.add))
                p.v([Bf("hl_s"), Bf("hh1")], [Bf("cwt_s")],
                    lambda e, sub=sub, et=et: e.tensor_tensor(
                        cwt_s[:, sub * 8:(sub + 1) * 8], hl_s[:, 0:8], et,
                        OP.mult))
            for sub in range(nsub):
                p.t([Bf("cwt_s"), Bf("sm_eye128")], [Bf("pb6")],
                    lambda e, sub=sub: e.transpose(
                        pb[6][0:8, 0:128],
                        cwt_s[:, sub * 8:(sub + 1) * 8], sml["eye128"]))
                p.v([Bf("pb6")], [Bf("cwem")],
                    lambda e, sub=sub: e.tensor_copy(
                        cwem[:, sub * 128:(sub + 1) * 128],
                        pb[6][0:8, 0:128]))

            for ex in range(9):
                eu = u0 + 8 + ex * 10
                routed = ex < E
                if routed:
                    p.t([Bf("cwem"), Bf("sm_ind")], [Bf("pb6")],
                        lambda e, ex=ex: e.matmul(
                            pb[6][:, 0:TT],
                            sml["ind"][:, ex * 128:(ex + 1) * 128], cwem,
                            start=True, stop=True))
                    p.v([Bf("pb6")], [Bf("cwb_sb")],
                        lambda e: e.tensor_copy(cwb_sb, pb[6][:, 0:TT]))
                boff = ex * 16
                for pr in range(8):
                    u = eu + pr
                    want_unit(u)
                    rb, rbuf = unit_buf(u)
                    mm3(0, 1, rb, rbuf, 0 * 1024, pLh, pLl, [Bf("pL")])
                    mm3(2, 3, rb, rbuf, 2 * 1024, pLh, pLl, [Bf("pL")])
                    mm3(4, 5, rb, rbuf, 1 * 1024, pLh, pLl, [Bf("pL")])
                    mm3(6, 7, rb, rbuf, 3 * 1024, pLh, pLl, [Bf("pL")])
                    evac3(0, 1, hh1, Bf("hh1"),
                          bias=sml["b1c"][:, boff + 2 * pr:boff + 2 * pr + 1],
                          bias_b=Bf("sm_b1c"))
                    evac3(2, 3, hh2, Bf("hh2"),
                          bias=sml["b3c"][:, boff + 2 * pr:boff + 2 * pr + 1],
                          bias_b=Bf("sm_b3c"))
                    p.v([Bf("hh1"), Bf("hh2")], [Bf("hg_s")],
                        lambda e: e.tensor_tensor(hg_s, hh1, hh2, OP.mult))
                    evac3(4, 5, hh1, Bf("hh1"),
                          bias=sml["b1c"][:, boff + 2 * pr + 1:
                                          boff + 2 * pr + 2],
                          bias_b=Bf("sm_b1c"))
                    evac3(6, 7, hh2, Bf("hh2"),
                          bias=sml["b3c"][:, boff + 2 * pr + 1:
                                          boff + 2 * pr + 2],
                          bias_b=Bf("sm_b3c"))
                    p.v([Bf("hh1"), Bf("hh2")], [Bf("hl_s")],
                        lambda e: e.tensor_tensor(hl_s, hh1, hh2, OP.mult))
                    p.v([Bf("hg_s")], [Bf("hg_s")],
                        lambda e: e.tensor_scalar(
                            hg_s, hg_s, LIMIT, None, OP.min))
                    p.s([Bf("hg_s")], [Bf("hh1")],
                        lambda e: e.activation(
                            out=hh1, in_=hg_s, func=AF.Sigmoid,
                            scale=ALPHA))
                    p.v([Bf("hg_s"), Bf("hh1")], [Bf("hg_s")],
                        lambda e: e.tensor_tensor(hg_s, hg_s, hh1,
                                                  OP.mult))
                    p.v([Bf("hl_s")], [Bf("hl_s")],
                        lambda e: e.tensor_scalar(
                            hl_s, hl_s, -LIMIT, LIMIT, OP.max, OP.min))
                    p.v([Bf("hl_s"), Bf("hg_s")], [Bf("c32a")],
                        lambda e: e.scalar_tensor_tensor(
                            c32a, hl_s, 1.0, hg_s, OP.add, OP.mult))
                    if routed:
                        p.v([Bf("c32a"), Bf("cwb_sb")], [Bf("c32a")],
                            lambda e: e.tensor_tensor(c32a, c32a, cwb_sb,
                                                      OP.mult))
                    split_chunk(c32a, Bf("c32a"), chunk(pGh, pr),
                                chunk(pGl, pr), Bf("pG"), c32b, Bf("c32b"))
                for dc in range(KC):
                    u = eu + 8 + dc // 4
                    want_unit(u)
                    rb, rbuf = unit_buf(u)
                    base = (dc % 4) * 1024
                    a_i, b_i = (0, 1) if dc % 2 == 0 else (2, 3)
                    mm3(a_i, b_i, rb, rbuf, base, pGh, pGl, [Bf("pG")])
                    evac3(a_i, b_i, hh2, Bf("hh2"))
                    if routed:
                        if ex == 0:
                            p.v([Bf("hh2"), Bf("cwb_sb"), Bf("sm_b2c")],
                                [Bf("acc")],
                                lambda e, dc=dc, ex=ex:
                                e.scalar_tensor_tensor(
                                    chunk(acc, dc), cwb_sb,
                                    sml["b2c"][:, ex * 8 + dc:
                                               ex * 8 + dc + 1],
                                    hh2, OP.mult, OP.add))
                        else:
                            p.v([Bf("hh2"), Bf("cwb_sb"), Bf("sm_b2c")],
                                [Bf("hh1")],
                                lambda e, dc=dc, ex=ex:
                                e.scalar_tensor_tensor(
                                    hh1, cwb_sb,
                                    sml["b2c"][:, ex * 8 + dc:
                                               ex * 8 + dc + 1],
                                    hh2, OP.mult, OP.add))
                            p.v([Bf("hh1"), Bf("acc")], [Bf("acc")],
                                lambda e, dc=dc: e.tensor_tensor(
                                    chunk(acc, dc), chunk(acc, dc), hh1,
                                    OP.add))
                    else:
                        p.v([Bf("hh2"), Bf("acc"), Bf("sm_sb2c")],
                            [Bf("acc")],
                            lambda e, dc=dc: e.scalar_tensor_tensor(
                                chunk(acc, dc), hh2,
                                sml["sb2c"][:, dc:dc + 1],
                                chunk(acc, dc), OP.add, OP.add))

            for kc in range(KC):
                p.v([Bf("x_t"), Bf("acc")], [Bf("acc")],
                    lambda e, kc=kc: e.tensor_tensor(
                        chunk(acc, kc), chunk(x_t, kc), chunk(acc, kc),
                        OP.add))
            p.dma("g", "qy", [Bf("acc")], [Bf("y_out")],
                  lambda e, t0=t0: e.dma_start(
                      out=y_h.rearrange("(k p) t -> p k t",
                                        k=KC)[:, :, t0:t0 + TT],
                      in_=acc.rearrange("p (k t) -> p k t", k=KC)))

        final = p.cnt["qy"]
        p.raw("g", lambda e: e.wait_ge(p.sems["qy"], final))

        @block.tensor
        def _(eng):
            for f in p.streams["t"]:
                f(eng)

        @block.vector
        def _(eng):
            for f in p.streams["v"]:
                f(eng)

        @block.scalar
        def _(eng):
            for f in p.streams["s"]:
                f(eng)

        @block.sync
        def _(eng):
            for f in p.streams["y"]:
                f(eng)

        @block.gpsimd
        def _(eng):
            for f in p.streams["g"]:
                f(eng)

    return nc


def kernel(**inputs):
    T = inputs["L0_attn_x"].shape[1]
    x = np.asarray(inputs["L0_attn_x"], np.float32)
    nb = x.shape[0]
    wpack = build_wpack({k: np.asarray(v, np.float32)
                         for k, v in inputs.items()})
    smalls = build_smalls({k: np.asarray(v, np.float32)
                           for k, v in inputs.items()})
    nc = build_program(T)
    in_maps = []
    for b in range(nb):
        m = {"x": np.ascontiguousarray(x[b].T), "wpack": wpack}
        m.update(smalls)
        in_maps.append(m)
    res = run_bass_kernel_spmd(nc, in_maps, core_ids=list(range(nb)))
    out = np.stack([np.asarray(r["y"]).T for r in res.results])
    return np.ascontiguousarray(out.astype(np.float32))
